# version 15 (final)
# baseline (speedup 1.0000x reference)
import contextlib

import numpy as np

import concourse.bass as bass
import concourse.mybir as mybir
from concourse.bass_isa_rust import ReduceOp
from concourse.bass_utils import run_bass_kernel_spmd

F32 = mybir.dt.float32
BF16 = mybir.dt.bfloat16
I8 = mybir.dt.int8
I32 = mybir.dt.int32
AF = mybir.ActivationFunctionType
ALU = mybir.AluOpType

REAL_CFG = dict(D=2048, DI=4096, NST=16, RK=128, KCV=4, L=8192, H=8192, NC=8,
                TG=2048, HB=128, ARKB=166)
DEBUG = False


class Sched:
    ENGINES = ("T", "V", "A", "S", "G")
    NPOOL = {"S": 10, "G": 6, "A": 2, "CC": 3}

    def __init__(self):
        self.strict = set()
        self.ops = {e: [] for e in self.ENGINES}
        self.counts = {k: 0 for k in ("T", "V", "A", "S", "G")}
        for q, n in self.NPOOL.items():
            for i in range(n):
                self.counts[f"{q}#{i}"] = 0
        self.rr = {q: 0 for q in self.NPOOL}

    def op(self, eng, emit, deps=(), token=False, dma=False, cc=False,
           strict=False):
        deps = J(deps)
        if dma or cc:
            q = "CC" if cc else eng
            idx = self.rr[q]
            self.rr[q] = (idx + 1) % self.NPOOL[q]
            stream = f"{q}#{idx}"
            if self.counts[stream] > 0:
                deps.append((stream, self.counts[stream]))
            self.counts[stream] += 1
            tok = (stream, self.counts[stream])
        else:
            stream = eng
            tok = None
            if token:
                self.counts[stream] += 1
                tok = (stream, self.counts[stream])
                if strict:
                    self.strict.add(tok)
        self.ops[eng].append((emit, tuple(deps), tok, stream))
        return tok

    def barrier_tokens(self):
        return [(k, v) for k, v in self.counts.items() if v > 0]

    def drain_tokens(self):
        return [(k, v) for k, v in self.counts.items()
                if v > 0 and "#" in k]

    def finalize(self, nc, block):
        stack = contextlib.ExitStack()
        sems = {name: stack.enter_context(nc.semaphore("sem_" + name))
                for name in self.counts}
        eng_attr = dict(T="tensor", V="vector", A="scalar", S="sync", G="gpsimd")

        def run_engine(eng_name, eng):
            satisfied = {}
            for emit, deps, tok, stream in self.ops[eng_name]:
                need = {}
                for (s, v) in deps:
                    if v > need.get(s, 0):
                        need[s] = v
                for s in sorted(need):
                    v = need[s]
                    if satisfied.get(s, 0) < v:
                        mult = 16 if ("#" in s and not s.startswith("CC")) else 1
                        eng.wait_ge(sems[s], v * mult)
                        satisfied[s] = v
                inst = emit(eng)
                if tok is not None:
                    mult = 16 if ("#" in stream and not stream.startswith("CC")) else 1
                    inst.then_inc(sems[stream], mult)

        for e in self.ENGINES:
            def make(name):
                def body(eng):
                    run_engine(name, eng)
                return body
            getattr(block, eng_attr[e])(make(e))
        return stack


def J(*toks):
    out = []
    for t in toks:
        if t is None:
            continue
        if isinstance(t, tuple) and len(t) == 2 and isinstance(t[0], str):
            out.append(t)
        else:
            out.extend(J(*t))
    return out


class Arena:

    def __init__(self, base_ap, nbytes):
        self.base = base_ap
        self.nbytes = nbytes
        self.off = 0
        self.limit = nbytes

    def reset(self, limit=None):
        self.off = 0
        self.limit = limit if limit is not None else self.nbytes

    def raw(self, nb):
        off = -(-self.off // 256) * 256
        assert off + nb <= self.limit, f"arena overflow {off}+{nb}>{self.limit}"
        self.off = off + nb
        return self.base[:, off:off + nb]

    def alloc(self, dtype, free_elems):
        return self.raw(free_elems * mybir.dt.size(dtype)).bitcast(dtype)

    def at(self, off_b, dtype, free_elems):
        nb = free_elems * mybir.dt.size(dtype)
        assert off_b + nb <= self.nbytes
        return self.base[:, off_b:off_b + nb].bitcast(dtype)


def build(cfg):
    D = cfg["D"]; DI = cfg["DI"]; NST = cfg["NST"]; RK = cfg["RK"]
    KCV = cfg["KCV"]; L = cfg["L"]; H = cfg["H"]; NC = cfg["NC"]
    TG = cfg["TG"]; HB = cfg["HB"]; ARKB = cfg["ARKB"]
    TL = L // NC
    DSL = DI // NC
    NCH = L // TG
    TCL = TG // NC
    DCH = D // 128
    GS = DSL // 128
    HCH = H // 128
    NHB = H // HB
    PSB = 512
    TH = TL // 2
    NBW = max(1, D // PSB)
    WW = min(PSB, D)
    EPS_LN = 1e-5
    EPS_RMS_F = 1e-8
    EPS_RMS_A = 1e-5
    ACT_EPS = 1e-4
    XPW = RK + 2 * NST
    XW = TG + KCV - 1

    nc = bass.Bass("TRN2", target_bir_lowering=False, debug=False,
                   detect_race_conditions=False)

    def dram_in(name, shape):
        return nc.dram_tensor(name, list(shape), F32, kind="ExternalInput")

    x_in = dram_in("x_sl", [D, TL])
    w1t = dram_in("w1t", [D, H]); w2t = dram_in("w2t", [D, H])
    w3t = dram_in("w3t", [H, D])
    s1rows = dram_in("s1rows", [D // NC, H])
    s2rows = dram_in("s2rows", [D // NC, H])
    s3rows = dram_in("s3rows", [H // NC, D])
    winsl = dram_in("winsl", [D, 2 * DSL])
    woutsl = dram_in("woutsl", [DSL, D])
    wxpsl = dram_in("wxpsl", [DSL, XPW])
    wdtsl = dram_in("wdtsl", [RK, DSL])
    convwsl = dram_in("convwsl", [128, GS * KCV])
    convbsl = dram_in("convbsl", [128, GS])
    alogsl = dram_in("alogsl", [128, GS * NST])
    ddsl = dram_in("ddsl", [128, GS])
    bdtsl = dram_in("bdtsl", [128, GS])
    wnormv = dram_in("wnormv", [128, DCH])
    g1v = dram_in("g1v", [128, DCH]); g2v = dram_in("g2v", [128, DCH])
    g3v = dram_in("g3v", [128, HCH])
    identin = dram_in("identin", [128, 128])
    constv = dram_in("constv", [128, 8])
    out_t = nc.dram_tensor("out_sl", [D, TL], F32, kind="ExternalOutput")
    dbg_t = (nc.dram_tensor("dbg", [128, 128], F32, kind="ExternalOutput")
             if DEBUG else None)
    dbg_sb = None
    dbg_i = [0]

    w1q_sc = nc.dram_tensor("w1q_sc", [NHB, 128, DCH * HB], BF16)
    w2q_sc = nc.dram_tensor("w2q_sc", [NHB, 128, DCH * HB], BF16)
    w3q_sc = nc.dram_tensor("w3q_sc", [DCH, H, 128], BF16)
    h_sc = nc.dram_tensor("h_sc", [H, TL], F32)
    scr_kind = dict(kind="ExternalOutput") if DEBUG else {}
    x1_sc = nc.dram_tensor("x1_sc", [D, TL], F32, **scr_kind)
    x2_sc = nc.dram_tensor("x2_sc", [D, TL], F32, **scr_kind)
    ag_in = nc.dram_tensor("ag_in", [D, TCL], BF16)
    ag_out = nc.dram_tensor("ag_out", [NC * D, TCL], BF16)
    ar_in = nc.dram_tensor("ar_in", [XPW, TG], F32)
    ar_out = nc.dram_tensor("ar_out", [XPW, TG], F32)
    rs_in = nc.dram_tensor("rs_in", [TG, D], F32)
    rs_out = nc.dram_tensor("rs_out", [TCL, D], F32)
    if DEBUG:
        ridump = nc.dram_tensor("ridump", [TG, D], F32, kind="ExternalOutput")
        rodump = nc.dram_tensor("rodump", [TCL, D], F32, kind="ExternalOutput")
        aidump = nc.dram_tensor("aidump", [D, TCL], BF16, kind="ExternalOutput")
        ybdump = nc.dram_tensor("ybdump", [128, GS * TG], BF16,
                                kind="ExternalOutput")
        xpdump = nc.dram_tensor("xpdump", [XPW, TG], F32, kind="ExternalOutput")
        xodump = nc.dram_tensor("xodump", [XPW, TG], F32, kind="ExternalOutput")
        xsdump = nc.dram_tensor("xsdump", [128, GS * TG], BF16,
                                kind="ExternalOutput")
        u2dump = nc.dram_tensor("u2dump", [128, GS * TG], BF16,
                                kind="ExternalOutput")
        sr2dump = nc.dram_tensor("sr2dump", [128, GS * TG], BF16,
                                 kind="ExternalOutput")
        uddump = nc.dram_tensor("uddump", [128, GS * TG], BF16,
                                kind="ExternalOutput")
        agdump = nc.dram_tensor("agdump", [NC * D, TCL], BF16,
                                kind="ExternalOutput")
    win_bf = nc.dram_tensor("win_bf", [D, 2 * DSL], BF16)
    wout_bf = nc.dram_tensor("wout_bf", [DSL, D], BF16)
    wxp_bf = nc.dram_tensor("wxp_bf", [DSL, XPW], BF16)
    sct_in = nc.dram_tensor("sct_in", [1, 8], F32)
    sct_out = nc.dram_tensor("sct_out", [1, 8], F32)

    RG = [list(range(NC))]
    sch = Sched()
    E = sch.op
    stack = contextlib.ExitStack()

    def sb(name, p, free, dtype):
        return stack.enter_context(nc.sbuf_tensor(name, [p, free], dtype)).ap()

    xres = sb("xres", 128, DCH * TL, BF16)
    wxp_b = sb("wxp_b", 128, GS * XPW, BF16)
    wdt_b = sb("wdt_b", 128, DSL, F32)
    convw_s = sb("convw_s", 128, GS * KCV, F32)
    convb_s = sb("convb_s", 128, GS, F32)
    acoef = sb("acoef", 128, GS * NST, F32)
    dvec_s = sb("dvec_s", 128, GS, F32)
    bdt_s = sb("bdt_s", 128, GS, F32)
    wnorm_s = sb("wnorm_s", 128, DCH, F32)
    g1_s = sb("g1_s", 128, DCH, F32)
    g2_s = sb("g2_s", 128, DCH, F32)
    g3_s = sb("g3_s", 128, HCH, F32)
    state = sb("state", 128, GS * NST, F32)
    xs_tail = sb("xs_tail", 128, GS * (KCV - 1), BF16)
    scq = sb("scq", 128, 12, F32)
    sstats = sb("sstats", 128, 8, F32)
    identf = sb("identf", 128, 128, F32)
    onesf = sb("onesf", 128, 128, F32)
    dance = sb("dance", 128, 192, F32)
    mflat = sb("mflat", 1, TL, F32)
    const_s = sb("const_s", 128, 8, F32)
    CONST_VALS = [EPS_LN, EPS_RMS_F, 1.0, -0.5, 1.0 / H, -1.0, 0.5, 2.0]
    for ci, cv in enumerate(CONST_VALS):
        nc.const_aps.register(F32, cv, const_s[:, ci:ci + 1])

    if DEBUG:
        dbg_sb = sb("dbg_sb", 128, 96, F32)
        sch.op("V", lambda v: v.memset(dbg_sb, 0.0), token=True)

    def DBG(ap_src, deps):
        if not DEBUG or dbg_i[0] >= 12:
            return None
        i = dbg_i[0]; dbg_i[0] += 1
        return E("V", (lambda i_, s_: lambda v: v.tensor_copy(
            out=dbg_sb[:, i_ * 8:(i_ + 1) * 8], in_=s_))(i, ap_src),
            deps=deps, token=True)

    ARB = ARKB * 1024 - 512 - (512 if DEBUG else 0)
    arena_t = sb("arena", 128, ARB, I8)
    AR = Arena(arena_t, ARB)
    TOPB = 72 * 1024
    TOP0 = AR.nbytes - TOPB

    pb = [stack.enter_context(nc.psum_tensor("pb%d" % i, [128, PSB], F32)).ap()
          for i in range(8)]

    t_wdt = E("G", lambda g: g.dma_start(out=wdt_b, in_=wdtsl.ap()), dma=True)
    for dst, src in ((convw_s, convwsl), (convb_s, convbsl), (dvec_s, ddsl),
                     (bdt_s, bdtsl), (wnorm_s, wnormv), (g1_s, g1v),
                     (g2_s, g2v), (g3_s, g3v)):
        E("S", (lambda d_, s_: lambda e: e.dma_start(out=d_, in_=s_.ap()))(dst, src),
          dma=True)
    t_alog = E("S", lambda e: e.dma_start(out=acoef, in_=alogsl.ap()), dma=True)
    E("S", lambda e: e.dma_start(out=const_s, in_=constv.ap()), dma=True)
    AR.reset()
    t_id = E("S", lambda e: e.dma_start(out=identf, in_=identin.ap()), dma=True)
    t_idb = t_id
    t_ones = E("V", lambda v: v.memset(onesf, 1.0), token=True)
    t_a1 = E("A", lambda a: a.activation(out=acoef, in_=acoef, func=AF.Exp),
             deps=[t_alog], token=True)
    t_acoef = E("V", lambda v: v.tensor_scalar_mul(out=acoef, in0=acoef,
                scalar1=-1.0), deps=[t_a1], token=True)
    E("V", lambda v: v.memset(state, 0.0), token=True)
    t_tail0 = E("V", lambda v: v.memset(xs_tail, 0.0), token=True)

    AR.reset()
    cvt_f = [AR.alloc(F32, D) for _ in range(2)]
    cvt_b = [AR.alloc(BF16, D) for _ in range(2)]
    cvt_free = [[], []]
    cvt_last = []
    cbi = 0
    for (srcd, dstd, rows, cols) in ((winsl, win_bf, D, 2 * DSL),
                                     (woutsl, wout_bf, DSL, D),
                                     (wxpsl, wxp_bf, DSL, XPW)):
        for r0 in range(0, rows, 128):
            sl = cbi % 2
            cbi += 1
            d_i = E("S", (lambda s_, r_, c_, i_: lambda e: e.dma_start(
                out=cvt_f[i_][:, 0:c_], in_=s_.ap()[r_:r_ + 128, :]))(
                srcd, r0, cols, sl), deps=cvt_free[sl], dma=True)
            t_c = E("V", (lambda c_, i_: lambda v: v.tensor_copy(
                out=cvt_b[i_][:, 0:c_], in_=cvt_f[i_][:, 0:c_]))(cols, sl),
                deps=[d_i], token=True)
            d_o = E("G", (lambda d_, r_, c_, i_: lambda e: e.dma_start(
                out=d_.ap()[r_:r_ + 128, :], in_=cvt_b[i_][:, 0:c_]))(
                dstd, r0, cols, sl), deps=[t_c], dma=True)
            cvt_free[sl] = [t_c, d_o]
            cvt_last = [t_c, d_o]
    t_wxp = E("G", lambda g: g.dma_start(
        out=wxp_b, in_=wxp_bf.ap().rearrange("(g p) r -> p g r", p=128)),
        deps=cvt_last, dma=True)

    srow = AR.alloc(F32, H)
    tmpc = AR.alloc(F32, 1)
    ath_t = AR.alloc(F32, 4)
    t_set = E("V", lambda v: v.memset(sstats, 0.0), token=True)
    red_tok = t_set
    first_dma = True
    for (src, rows, cols, cidx) in ((s1rows, D // NC, H, 0),
                                    (s2rows, D // NC, H, 1),
                                    (s3rows, H // NC, D, 2)):
        for r0 in range(0, rows, 128):
            rp = min(128, rows - r0)
            tok_d = E("S", (lambda s_, r_, rp_, c_: lambda e: e.dma_start(
                out=srow[0:rp_, :c_], in_=s_.ap()[r_:r_ + rp_, :]))(src, r0, rp, cols),
                deps=cvt_last if first_dma else [red_tok], dma=True)
            first_dma = False
            t_r = E("V", (lambda rp_, c_: lambda v: v.tensor_reduce(
                out=tmpc[0:rp_, :], in_=srow[0:rp_, :c_], axis=mybir.AxisListType.X,
                op=ALU.add, apply_absolute_value=True))(rp, cols),
                deps=[tok_d], token=True, strict=True)
            red_tok = E("V", (lambda rp_, ci: lambda v: v.tensor_tensor(
                out=sstats[0:rp_, ci:ci + 1], in0=sstats[0:rp_, ci:ci + 1],
                in1=tmpc[0:rp_, :], op=ALU.add))(rp, cidx),
                deps=[t_r, red_tok], token=True)
    t_par0 = E("T", lambda t: t.matmul(pb[0][:, 0:3], onesf, sstats[:, 0:3],
               start=True, stop=True), deps=[red_tok, t_ones], token=True)
    t_par = E("V", lambda v: v.tensor_copy(out=sstats[:, 0:3], in_=pb[0][:, 0:3]),
              deps=[t_par0], token=True)
    t_w = E("S", lambda e: e.dma_start(out=sct_in.ap(), in_=sstats[0:1, 0:8]),
            deps=[t_par], dma=True)
    t_cc0 = E("G", lambda g: g.collective_compute(
        "AllReduce", ALU.add, replica_groups=RG,
        ins=[sct_in.ap().opt()], outs=[sct_out.ap().opt()]), deps=[t_w], cc=True)
    t_rb = E("S", lambda e: e.dma_start(out=sstats[0:1, 0:8], in_=sct_out.ap()),
             deps=[t_cc0], dma=True)
    t_pb0 = E("T", lambda t: t.matmul(pb[0][:, 0:8], onesf[0:1, :],
              sstats[0:1, 0:8], start=True, stop=True),
              deps=[t_rb, t_par], token=True)
    t_pbr = E("V", lambda v: v.tensor_copy(out=sstats[:, 0:8], in_=pb[0][:, 0:8]),
              deps=[t_pb0], token=True)
    t_scales = []
    for m, nelem in ((0, D * H), (1, D * H), (2, H * D)):
        t1 = E("V", (lambda m_, n_: lambda v: v.tensor_scalar(
            out=scq[:, m_:m_ + 1], in0=sstats[:, m_:m_ + 1],
            scalar1=1.0 / n_, scalar2=EPS_RMS_F, op0=ALU.mult, op1=ALU.add))(m, nelem),
            deps=[t_pbr], token=True)
        t1b = E("A", (lambda m_: lambda a: a.activation(
            out=scq[:, m_:m_ + 1], in_=scq[:, m_:m_ + 1], func=AF.Ln))(m),
            deps=[t1], token=True)
        t2 = E("A", (lambda m_: lambda a: a.activation(
            out=scq[:, m_:m_ + 1], in_=scq[:, m_:m_ + 1], func=AF.Exp,
            scale=-1.0))(m), deps=[t1b], token=True)
        tn0 = E("V", (lambda m_, n_: lambda v: v.tensor_scalar(
            out=ath_t[:, 3:4], in0=sstats[:, m_:m_ + 1],
            scalar1=1.0 / n_, scalar2=EPS_RMS_F, op0=ALU.mult, op1=ALU.add))(
            m, nelem), deps=[t_pbr, t2], token=True)
        tn1 = E("V", (lambda m_: lambda v: v.tensor_tensor(
            out=ath_t[:, 3:4], in0=ath_t[:, 3:4], in1=scq[:, m_:m_ + 1],
            op=ALU.mult))(m), deps=[tn0, t2], token=True)
        tn2 = E("V", lambda v: v.tensor_scalar(
            out=ath_t[:, 3:4], in0=ath_t[:, 3:4], scalar1=-1.0, scalar2=2.0,
            op0=ALU.mult, op1=ALU.add), deps=[tn1], token=True)
        t2 = E("V", (lambda m_: lambda v: v.tensor_tensor(
            out=scq[:, m_:m_ + 1], in0=scq[:, m_:m_ + 1], in1=ath_t[:, 3:4],
            op=ALU.mult))(m), deps=[tn2], token=True)
        t2k = E("V", (lambda m_: lambda v: v.tensor_scalar_mul(
            out=scq[:, 6 + m_:7 + m_], in0=scq[:, m_:m_ + 1],
            scalar1=0.910239226626837))(m), deps=[t2], token=True)
        t3 = E("V", (lambda m_, n_: lambda v: v.tensor_scalar_mul(
            out=ath_t[:, m_:m_ + 1], in0=sstats[:, m_:m_ + 1], scalar1=1.0 / n_))(
            m, nelem), deps=[t_pbr], token=True)
        sc_ = scq[:, 3 + m:4 + m]
        tc_ = ath_t[:, 3:4]
        s_ = ath_t[:, m:m + 1]
        tprev = E("V", (lambda s1, d_: lambda v: v.tensor_tensor(
            out=d_, in0=s1, in1=s1, op=ALU.mult))(s_, tc_), deps=[t3], token=True)
        t9 = E("V", (lambda d_, t_: lambda v: v.tensor_scalar(
            out=d_, in0=t_, scalar1=1.0 / 9.0, scalar2=1.0 / 7.0,
            op0=ALU.mult, op1=ALU.add))(sc_, tc_), deps=[tprev], token=True)
        for cc_ in (1.0 / 5.0, 1.0 / 3.0, 1.0):
            t9 = E("V", (lambda d_, t_: lambda v: v.tensor_tensor(
                out=d_, in0=d_, in1=t_, op=ALU.mult))(sc_, tc_),
                deps=[t9], token=True)
            t9 = E("V", (lambda d_, c2: lambda v: v.tensor_scalar_add(
                out=d_, in0=d_, scalar1=c2))(sc_, cc_), deps=[t9], token=True)
        t9 = E("V", (lambda d_, s1: lambda v: v.tensor_tensor(
            out=d_, in0=d_, in1=s1, op=ALU.mult))(sc_, s_), deps=[t9], token=True)
        t_scales += [t2, t2k, t9]
    DBG(sstats[:, 0:8], [t_pbr])
    DBG(scq[:, 0:8], t_scales)


    def rsqrt_bc(dst, src, scale, bias, deps):
        t1 = E("A", (lambda d_, s_, sc_, b_: lambda a: a.activation(
            out=d_, in_=s_, func=AF.Ln, scale=sc_, bias=b_))(dst, src, scale, bias),
            deps=deps, token=True)
        return E("A", (lambda d_: lambda a: a.activation(
            out=d_, in_=d_, func=AF.Exp, scale=-0.5))(dst), deps=[t1], token=True)

    dance_free = [None]

    def sum_bc(dst, partials, T_, deps, banks, bank_free):
        last = None
        for i in range((T_ + 511) // 512):
            w = min(512, T_ - i * 512)
            bk = banks[i % len(banks)]
            mm = E("T", (lambda b_, i_, w_, p_: lambda t: t.matmul(
                pb[b_][:, 0:w_], onesf, p_[:, i_ * 512:i_ * 512 + w_],
                start=True, stop=True))(bk, i, w, partials),
                deps=J(deps, [bank_free[bk], t_ones]), token=True)
            last = E("V", (lambda b_, i_, w_, d_: lambda v: v.tensor_copy(
                out=d_[:, i_ * 512:i_ * 512 + w_], in_=pb[b_][:, 0:w_]))(bk, i, w, dst),
                deps=[mm], token=True)
            bank_free[bk] = last
        return last

    def max_bc(dst, partials, T_, deps, banks, bank_free):
        NT_ = T_ // 128
        bA, bB = banks[0], banks[1 % len(banks)]
        mcol = dance[:, 0:NT_]
        mrow = dance[0:NT_, 64:192]
        rtoks = []
        for j in range(NT_):
            bk = bA if j % 2 == 0 else bB
            tr = E("T", (lambda b_, j_, p_: lambda t: t.transpose(
                pb[b_][:, 0:128], p_[:, j_ * 128:(j_ + 1) * 128], identf))(
                bk, j, partials),
                deps=J(deps, [bank_free[bk], t_id], dance_free), token=True)
            rd = E("V", (lambda b_, j_: lambda v: v.reduce_max(
                out=mcol[:, j_:j_ + 1], in_=pb[b_][:, 0:128],
                axis=mybir.AxisListType.X))(bk, j), deps=[tr], token=True,
                strict=True)
            bank_free[bk] = rd
            rtoks.append(rd)
        tr2 = E("T", (lambda: lambda t: t.transpose(
            pb[bA][0:NT_, 0:128], mcol, identf))(),
            deps=J(rtoks, [bank_free[bA]]), token=True)
        ev2 = E("V", (lambda: lambda v: v.tensor_copy(
            out=mrow, in_=pb[bA][0:NT_, 0:128]))(), deps=[tr2], token=True)
        bank_free[bA] = ev2
        dm = E("S", (lambda: lambda e: e.dma_start(
            out=mflat[0:1, 0:T_], in_=mrow))(), deps=[ev2], dma=True)
        last = None
        for i in range((T_ + 511) // 512):
            w = min(512, T_ - i * 512)
            bk = banks[i % len(banks)]
            mm = E("T", (lambda b_, i_, w_: lambda t: t.matmul(
                pb[b_][:, 0:w_], onesf[0:1, :], mflat[0:1, i_ * 512:i_ * 512 + w_],
                start=True, stop=True))(bk, i, w),
                deps=J([dm], [bank_free[bk]]), token=True)
            last = E("V", (lambda b_, i_, w_, d_: lambda v: v.tensor_copy(
                out=d_[:, i_ * 512:i_ * 512 + w_], in_=pb[b_][:, 0:w_]))(bk, i, w, dst),
                deps=[mm], token=True)
            bank_free[bk] = last
        dance_free[0] = last
        return last

    def ffn_sublayer(call, entry_deps, gate, final_dma=False):
        deps0 = J(entry_deps)
        bank_free = [None] * 8
        xq1 = AR.at(TOP0, BF16, DCH * TL)
        xq2 = AR.at(TOP0 + DCH * TL * 2, BF16, DCH * TL)
        q1i_bc = AR.at(TOP0 + 4 * DCH * TL, F32, TL)
        q2i_bc = AR.at(TOP0 + 4 * DCH * TL + 4 * TL, F32, TL)

        src_x = x_in if call == 0 else x2_sc
        AR.reset(limit=TOP0)
        xch = [AR.alloc(F32, TL) for _ in range(2)]
        zt = AR.alloc(F32, TL)
        i32t = AR.alloc(I32, TL)
        sqch = AR.alloc(F32, TL)
        stat1 = AR.alloc(F32, TL)
        stat2 = AR.alloc(F32, TL)
        mu_bc = AR.alloc(F32, TL)
        rstd_bc = AR.alloc(F32, TL)
        rho_bc = AR.alloc(F32, TL)
        q_bc = AR.alloc(F32, TL)
        m_bc = AR.alloc(F32, TL)

        acc1 = E("V", lambda v: v.memset(stat1, 0.0), deps=J(deps0, gate), token=True)
        acc2 = E("V", lambda v: v.memset(stat2, 0.0), deps=J(deps0, gate), token=True)
        xch_free = [[], []]
        t_sq_prev = None
        for ch in range(DCH):
            slx = ch % 2
            d_x = E("S", (lambda c_, s_: lambda e: e.dma_start(
                out=xch[s_], in_=src_x.ap()[c_ * 128:(c_ + 1) * 128, :]))(ch, slx),
                deps=J(xch_free[slx], gate, deps0), dma=True)
            t_sq = E("A", (lambda s_: lambda a: a.activation(
                out=sqch, in_=xch[s_], func=AF.Square))(slx),
                deps=J([d_x], [acc2]), token=True)
            acc1 = E("V", (lambda s_: lambda v: v.tensor_tensor(
                out=stat1, in0=stat1, in1=xch[s_], op=ALU.add))(slx),
                deps=[d_x, acc1], token=True)
            acc2 = E("V", lambda v: v.tensor_tensor(
                out=stat2, in0=stat2, in1=sqch, op=ALU.add),
                deps=[t_sq, acc2], token=True)
            xch_free[slx] = [acc1, t_sq]
        t_px = sum_bc(stat1, stat1, TL, [acc1], (0, 1), bank_free)
        t_px2 = sum_bc(stat2, stat2, TL, [acc2], (2, 3), bank_free)
        t_mu = E("V", lambda v: v.tensor_scalar_mul(
            out=mu_bc, in0=stat1, scalar1=1.0 / D), deps=[t_px], token=True)
        t_m2 = E("V", lambda v: v.tensor_tensor(
            out=stat1, in0=mu_bc, in1=mu_bc, op=ALU.mult), deps=[t_mu], token=True)
        t_var = E("V", lambda v: v.scalar_tensor_tensor(
            out=stat2, in0=stat2, scalar=1.0 / D, in1=stat1,
            op0=ALU.mult, op1=ALU.subtract), deps=[t_px2, t_m2], token=True)
        t_rstd = rsqrt_bc(rstd_bc, stat2, 1.0, EPS_LN, [t_var])
        t_r2 = E("V", lambda v: v.tensor_tensor(
            out=rho_bc, in0=rstd_bc, in1=rstd_bc, op=ALU.mult),
            deps=[t_rstd], token=True)
        t_r3 = E("V", lambda v: v.tensor_scalar(
            out=rho_bc, in0=rho_bc, scalar1=-EPS_LN, scalar2=1.0,
            op0=ALU.mult, op1=ALU.add), deps=[t_r2], token=True)
        t_rho = rsqrt_bc(rho_bc, rho_bc, 1.0, EPS_RMS_F, [t_r3])

        def quant(gv, xq, qi_bc, aidx):
            t_mz = E("V", lambda v: v.memset(m_bc, 0.0),
                     deps=J([t_px2]), token=True)
            mx = t_mz
            for ch in range(DCH):
                slx = ch % 2
                d_x = E("S", (lambda c_, s_: lambda e: e.dma_start(
                    out=xch[s_], in_=src_x.ap()[c_ * 128:(c_ + 1) * 128, :]))(ch, slx),
                    deps=J(xch_free[slx]), dma=True)
                t1 = E("V", (lambda s_: lambda v: v.tensor_tensor(
                    out=zt, in0=xch[s_], in1=mu_bc, op=ALU.subtract))(slx),
                    deps=[d_x, t_mu, mx], token=True)
                t3 = E("A", (lambda c_, g_: lambda a: a.activation(
                    out=zt, in_=zt, func=AF.Abs,
                    scale=g_[:, c_:c_ + 1]))(ch, gv), deps=[t1], token=True)
                mx = E("V", lambda v: v.tensor_tensor(
                    out=m_bc, in0=m_bc, in1=zt, op=ALU.max), deps=[t3, mx],
                    token=True)
                xch_free[slx] = [mx]
            t_pm = max_bc(m_bc, m_bc, TL, [mx], (0, 1), bank_free)
            t_c1 = E("V", lambda v: v.tensor_tensor(
                out=q_bc, in0=rstd_bc, in1=rho_bc, op=ALU.mult),
                deps=[t_rstd, t_rho], token=True)
            t_c2 = E("V", lambda v: v.tensor_tensor(
                out=m_bc, in0=m_bc, in1=q_bc, op=ALU.mult), deps=[t_pm, t_c1],
                token=True)
            t_c3 = E("V", lambda v: v.tensor_scalar_max(
                out=m_bc, in0=m_bc, scalar1=ACT_EPS), deps=[t_c2], token=True)
            t_qi = E("V", (lambda qi_, ai_: lambda v: v.scalar_tensor_tensor(
                out=qi_, in0=m_bc, scalar=1.0 / 127.0,
                in1=scq[:, 3 + ai_:4 + ai_].broadcast_to((128, TL)),
                op0=ALU.mult, op1=ALU.mult))(qi_bc, aidx),
                deps=J([t_c3], t_scales), token=True)
            t_sv = E("V", lambda v: v.tensor_copy(out=zt, in_=m_bc),
                     deps=[t_qi], token=True)
            t_c5a = E("A", lambda a: a.activation(
                out=m_bc, in_=m_bc, func=AF.Ln), deps=[t_sv], token=True)
            t_c5 = E("A", lambda a: a.activation(
                out=m_bc, in_=m_bc, func=AF.Exp, scale=-1.0),
                deps=[t_c5a], token=True)
            t_n1 = E("V", lambda v: v.tensor_tensor(
                out=zt, in0=zt, in1=m_bc, op=ALU.mult), deps=[t_c5], token=True)
            t_n2 = E("V", lambda v: v.tensor_scalar(
                out=zt, in0=zt, scalar1=-1.0, scalar2=2.0,
                op0=ALU.mult, op1=ALU.add), deps=[t_n1], token=True)
            t_n3 = E("V", lambda v: v.tensor_tensor(
                out=m_bc, in0=m_bc, in1=zt, op=ALU.mult), deps=[t_n2], token=True)
            t_q = E("V", lambda v: v.scalar_tensor_tensor(
                out=q_bc, in0=m_bc, scalar=127.0, in1=q_bc,
                op0=ALU.mult, op1=ALU.mult), deps=[t_n3, t_c1], token=True)
            t_xq = None
            for ch in range(DCH):
                slx = ch % 2
                d_x = E("S", (lambda c_, s_: lambda e: e.dma_start(
                    out=xch[s_], in_=src_x.ap()[c_ * 128:(c_ + 1) * 128, :]))(ch, slx),
                    deps=J(xch_free[slx]), dma=True)
                t1 = E("V", (lambda s_: lambda v: v.tensor_tensor(
                    out=zt, in0=xch[s_], in1=mu_bc, op=ALU.subtract))(slx),
                    deps=J([d_x, t_q], [t_xq]), token=True)
                t2 = E("A", (lambda c_, g_: lambda a: a.activation(
                    out=zt, in_=zt, func=AF.Copy,
                    scale=g_[:, c_:c_ + 1]))(ch, gv), deps=[t1], token=True)
                t3 = E("V", lambda v: v.tensor_tensor(
                    out=i32t, in0=zt, in1=q_bc, op=ALU.mult), deps=[t2], token=True)
                t_xq = E("V", (lambda c_, x_: lambda v: v.tensor_copy(
                    out=x_[:, c_ * TL:(c_ + 1) * TL], in_=i32t))(ch, xq),
                    deps=[t3], token=True)
                xch_free[slx] = [t_xq]
            return t_xq, t_qi

        t_q1, t_q1i = quant(g1_s, xq1, q1i_bc, 0)
        t_q2, t_q2i = quant(g2_s, xq2, q2i_bc, 1)
        quantA_done = [t_q1, t_q2, t_q1i, t_q2i]

        AR.reset(limit=TOP0)
        wfb = [AR.alloc(F32, DCH * HB) for _ in range(2)]
        ti32w = AR.alloc(I32, DCH * HB)
        wqb = [AR.alloc(BF16, DCH * HB) for _ in range(4)]
        w3f = AR.alloc(F32, D)
        w3qo = AR.alloc(BF16, D)
        ev1 = AR.alloc(F32, TL)
        s1h = AR.alloc(F32, TL)
        l2h = AR.alloc(F32, TL)
        hblk = [AR.alloc(F32, TL) for _ in range(2)]
        hsq = AR.alloc(F32, TL)
        habs = AR.alloc(F32, TL)
        sh2acc = AR.alloc(F32, TL)
        hmaxacc = AR.alloc(F32, TL)

        lin_gate = quantA_done
        t_z1 = E("V", lambda v: v.memset(sh2acc, 0.0), deps=lin_gate, token=True)
        t_z2 = E("V", lambda v: v.memset(hmaxacc, 0.0), deps=lin_gate, token=True)

        wq_free = [None] * 4
        wf_free = [None, None]
        ti32_free = [None]
        w3f_free = [None]
        w3qo_free = [None]
        ev_free = [None, None]
        s1_free = [None, None]
        l2_free = [None, None]
        hblk_free = [None, None]
        wq_written = []
        w3_written = []
        sh2_tok = t_z1
        hmax_tok = t_z2
        NW3 = HCH // NHB

        def ternarize(dst, srcb, n, scol, deps_in):
            t_t = E("V", (lambda s_, n_, c_: lambda v: v.tensor_single_scalar(
                out=ti32w[:, :n_], in_=s_[:, :n_], scalar=scq[:, 6 + c_:7 + c_],
                op=ALU.mult))(srcb, n, scol),
                deps=J(deps_in, t_scales, lin_gate, ti32_free), token=True)
            t_b = E("V", (lambda d_, n_: lambda v: v.tensor_scalar(
                out=d_, in0=ti32w[:, :n_], scalar1=1.0, scalar2=-1.0,
                op0=ALU.min, op1=ALU.max))(dst, n), deps=[t_t], token=True)
            ti32_free[0] = t_b
            return t_b

        NNB = TL // PSB
        for b in range(NHB):
            wsl = b % 2
            i1, i2 = (2 * b) % 4, (2 * b + 1) % 4
            if call == 0:
                d1 = E("S", (lambda b_, s_: lambda e: e.dma_start(
                    out=wfb[s_], in_=w1t.ap().rearrange("(c p) h -> p c h", p=128)
                    [:, :, b_ * HB:(b_ + 1) * HB]))(b, wsl),
                    deps=J([wf_free[wsl]], lin_gate), dma=True)
                t_wq1 = ternarize(wqb[i1], wfb[wsl], DCH * HB, 0,
                                  J([d1], [wq_free[i1]]))
                wf_free[wsl] = t_wq1
                d2 = E("G", (lambda b_, s_: lambda e: e.dma_start(
                    out=wfb[s_], in_=w2t.ap().rearrange("(c p) h -> p c h", p=128)
                    [:, :, b_ * HB:(b_ + 1) * HB]))(b, wsl),
                    deps=[wf_free[wsl]], dma=True)
                t_wq2 = ternarize(wqb[i2], wfb[wsl], DCH * HB, 1,
                                  J([d2], [wq_free[i2]]))
                wf_free[wsl] = t_wq2
                wq_written.append(E("G", (lambda b_, i_: lambda e: e.dma_start(
                    out=w1q_sc.ap()[b_], in_=wqb[i_]))(b, i1), deps=[t_wq1], dma=True))
                wq_written.append(E("A", (lambda b_, i_: lambda e: e.dma_start(
                    out=w2q_sc.ap()[b_], in_=wqb[i_]))(b, i2), deps=[t_wq2], dma=True))
                for j in range(NW3):
                    rb = b * NW3 + j
                    d3 = E("A", (lambda r_: lambda e: e.dma_start(
                        out=w3f, in_=w3t.ap()[r_ * 128:(r_ + 1) * 128, :]))(rb),
                        deps=J(w3f_free, lin_gate), dma=True)
                    t_w3 = ternarize(w3qo, w3f, D, 2, J([d3], w3qo_free))
                    w3f_free[0] = t_w3
                    dsw = E("A", (lambda r_: lambda e: e.dma_start(
                        out=w3q_sc.ap()[:, r_ * 128:(r_ + 1) * 128, :]
                        .rearrange("m p q -> p m q"),
                        in_=w3qo.rearrange("p (m q) -> p m q", m=DCH)))(rb),
                        deps=[t_w3], dma=True)
                    w3qo_free[0] = dsw
                    w3_written.append(dsw)
            else:
                t_wq1 = E("S", (lambda b_, i_: lambda e: e.dma_start(
                    out=wqb[i_], in_=w1q_sc.ap()[b_]))(b, i1),
                    deps=J([wq_free[i1]], lin_gate), dma=True)
                t_wq2 = E("G", (lambda b_, i_: lambda e: e.dma_start(
                    out=wqb[i_], in_=w2q_sc.ap()[b_]))(b, i2),
                    deps=J([wq_free[i2]], lin_gate), dma=True)

            hb_i = b % 2
            s_toks = {}
            h_toks = []
            for lin, wtok, wq_i, xq, qtok, qitok, qi in (
                    (1, t_wq1, i1, xq1, t_q1, t_q1i, q1i_bc),
                    (2, t_wq2, i2, xq2, t_q2, t_q2i, q2i_bc)):
                for nb in range(NNB):
                    bank = (b % 2) * 4 + (lin - 1) * 2 + (nb % 2)
                    mm = None
                    for kc in range(DCH):
                        dep = [wtok, qtok]
                        if kc == 0:
                            dep.append(bank_free[bank])
                        mm = E("T", (lambda bk, kc_, i_, x_, nb_: lambda t: t.matmul(
                            pb[bk], wqb[i_][:, kc_ * HB:(kc_ + 1) * HB],
                            x_[:, kc_ * TL + nb_ * PSB:kc_ * TL + (nb_ + 1) * PSB],
                            start=(kc_ == 0), stop=(kc_ == DCH - 1)))(
                            bank, kc, wq_i, xq, nb),
                            deps=dep, token=(kc == DCH - 1))
                    if lin == 1:
                        t_e = E("V", (lambda bk, nb_, q_: lambda v: v.tensor_tensor(
                            out=ev1[:, nb_ * PSB:(nb_ + 1) * PSB], in0=pb[bk],
                            in1=q_[:, nb_ * PSB:(nb_ + 1) * PSB], op=ALU.mult))(
                            bank, nb, qi), deps=J([mm, qitok], [ev_free[nb]]),
                            token=True)
                        bank_free[bank] = t_e
                        t_s = E("A", (lambda nb_: lambda a: a.activation(
                            out=s1h[:, nb_ * PSB:(nb_ + 1) * PSB],
                            in_=ev1[:, nb_ * PSB:(nb_ + 1) * PSB], func=AF.Silu))(nb),
                            deps=J([t_e], [s1_free[nb]]), token=True)
                        ev_free[nb] = t_s
                        s_toks[nb] = t_s
                    else:
                        t_e2 = E("V", (lambda bk, nb_, q_: lambda v: v.tensor_tensor(
                            out=l2h[:, nb_ * PSB:(nb_ + 1) * PSB], in0=pb[bk],
                            in1=q_[:, nb_ * PSB:(nb_ + 1) * PSB], op=ALU.mult))(
                            bank, nb, qi), deps=J([mm, qitok], [l2_free[nb]]),
                            token=True)
                        bank_free[bank] = t_e2
                        t_h = E("V", (lambda nb_, hi: lambda v: v.tensor_tensor(
                            out=hblk[hi][:, nb_ * PSB:(nb_ + 1) * PSB],
                            in0=s1h[:, nb_ * PSB:(nb_ + 1) * PSB],
                            in1=l2h[:, nb_ * PSB:(nb_ + 1) * PSB], op=ALU.mult))(
                            nb, hb_i),
                            deps=J([t_e2, s_toks[nb]],
                                   [hblk_free[hb_i]] if nb == 0 else []),
                            token=True)
                        s1_free[nb] = t_h
                        l2_free[nb] = t_h
                        h_toks.append(t_h)
            t_hsq = E("A", (lambda hi: lambda a: a.activation(
                out=hsq, in_=hblk[hi], func=AF.Square))(hb_i),
                deps=J(h_toks, [sh2_tok]), token=True)
            sh2_tok = E("V", lambda v: v.tensor_tensor(
                out=sh2acc, in0=sh2acc, in1=hsq, op=ALU.add),
                deps=[t_hsq, sh2_tok], token=True)
            t_ab = E("A", (lambda b_, hi: lambda a: a.activation(
                out=habs, in_=hblk[hi], func=AF.Abs,
                scale=g3_s[:, b_:b_ + 1]))(b, hb_i),
                deps=J(h_toks, [hmax_tok]), token=True)
            hmax_tok = E("V", lambda v: v.tensor_tensor(
                out=hmaxacc, in0=hmaxacc, in1=habs, op=ALU.max),
                deps=[t_ab, hmax_tok], token=True)
            hblk_free[hb_i] = E("A", (lambda b_, hi: lambda e: e.dma_start(
                out=h_sc.ap()[b_ * HB:(b_ + 1) * HB, :], in_=hblk[hi]))(b, hb_i),
                deps=[hmax_tok, sh2_tok], dma=True)
            wq_free[i1] = h_toks[-1]
            wq_free[i2] = h_toks[-1]
        h_written = [hblk_free[0], hblk_free[1]]
        lin_last = J([sh2_tok, hmax_tok], h_written, wq_written, w3_written)

        xq3 = AR.at(TOP0, BF16, HCH * TH)
        q3_bc = AR.at(TOP0 + 4 * DCH * TL, F32, TL)
        q3i_bc = AR.at(TOP0 + 4 * DCH * TL + 4 * TL, F32, TL)
        t_ps = sum_bc(sh2acc, sh2acc, TL, [sh2_tok], (0, 1), bank_free)
        t_pm = max_bc(hmaxacc, hmaxacc, TL, [hmax_tok], (2, 3), bank_free)
        t_rho3 = rsqrt_bc(q3_bc, sh2acc, 1.0 / H, EPS_RMS_F, J([t_ps], lin_last))
        t_m3 = E("V", lambda v: v.tensor_tensor(
            out=hmaxacc, in0=hmaxacc, in1=q3_bc, op=ALU.mult),
            deps=[t_pm, t_rho3], token=True)
        t_m3c = E("V", lambda v: v.tensor_scalar_max(
            out=hmaxacc, in0=hmaxacc, scalar1=ACT_EPS), deps=[t_m3], token=True)
        t_q3i = E("V", lambda v: v.scalar_tensor_tensor(
            out=q3i_bc, in0=hmaxacc, scalar=0.5 / 127.0,
            in1=scq[:, 5:6].broadcast_to((128, TL)), op0=ALU.mult, op1=ALU.mult),
            deps=J([t_m3c], t_scales, lin_last), token=True)
        t_sv3 = E("V", lambda v: v.tensor_copy(out=habs, in_=hmaxacc),
                  deps=[t_q3i, hmax_tok], token=True)
        t_mia = E("A", lambda a: a.activation(
            out=hmaxacc, in_=hmaxacc, func=AF.Ln), deps=[t_sv3], token=True)
        t_mi = E("A", lambda a: a.activation(
            out=hmaxacc, in_=hmaxacc, func=AF.Exp, scale=-1.0),
            deps=[t_mia], token=True)
        t_n31 = E("V", lambda v: v.tensor_tensor(
            out=habs, in0=habs, in1=hmaxacc, op=ALU.mult), deps=[t_mi],
            token=True)
        t_n32 = E("V", lambda v: v.tensor_scalar(
            out=habs, in0=habs, scalar1=-1.0, scalar2=2.0,
            op0=ALU.mult, op1=ALU.add), deps=[t_n31], token=True)
        t_n33 = E("V", lambda v: v.tensor_tensor(
            out=hmaxacc, in0=hmaxacc, in1=habs, op=ALU.mult), deps=[t_n32],
            token=True)
        t_q3 = E("V", lambda v: v.scalar_tensor_tensor(
            out=q3_bc, in0=hmaxacc, scalar=127.0, in1=q3_bc,
            op0=ALU.mult, op1=ALU.mult), deps=[t_n33, t_rho3], token=True)

        AR.reset(limit=TOP0)
        hrd = [AR.alloc(F32, 4 * TH) for _ in range(2)]
        w3qc = [AR.alloc(BF16, HCH * 128) for _ in range(2)]
        ti32q = AR.alloc(I32, TH)
        evo = AR.alloc(F32, TH)
        ost = [AR.alloc(F32, TH) for _ in range(2)]
        xst = [AR.alloc(F32, TH) for _ in range(2)]

        w3_gate = J(lin_last, [t_q3, t_q3i])
        hrd_free = [None, None]
        w3qc_free = [None, None]
        evo_free = [None]
        ost_free = [[], []]
        xst_free = [None, None]
        xq3_war = list(w3_gate)
        out_toks = []
        for th in range(2):
            xq3_last = None
            for hc4 in range(HCH // 4):
                sl = hc4 % 2
                d_h = E("S" if hc4 % 2 == 0 else "G",
                    (lambda h4, s_, th_: lambda e: e.dma_start(
                    out=hrd[s_], in_=h_sc.ap().rearrange("(c p) t -> p c t", p=128)
                    [:, h4 * 4:(h4 + 1) * 4, th_ * TH:(th_ + 1) * TH]))(hc4, sl, th),
                    deps=J([hrd_free[sl]], h_written, w3_gate), dma=True)
                for j in range(4):
                    hc = hc4 * 4 + j
                    t_v = E("A", (lambda s_, j_, hc_: lambda a: a.activation(
                        out=hrd[s_][:, j_ * TH:(j_ + 1) * TH],
                        in_=hrd[s_][:, j_ * TH:(j_ + 1) * TH],
                        func=AF.Copy, scale=g3_s[:, hc_:hc_ + 1]))(sl, j, hc),
                        deps=[d_h], token=True)
                    t_zq = E("V", (lambda s_, j_, th_: lambda v: v.tensor_tensor(
                        out=ti32q, in0=hrd[s_][:, j_ * TH:(j_ + 1) * TH],
                        in1=q3_bc[:, th_ * TH:(th_ + 1) * TH], op=ALU.mult))(sl, j, th),
                        deps=[t_v, t_q3, xq3_last], token=True)
                    xq3_last = E("V", (lambda hc_: lambda v: v.tensor_copy(
                        out=xq3[:, hc_ * TH:(hc_ + 1) * TH], in_=ti32q))(hc),
                        deps=J([t_zq], xq3_war if hc == 0 else []), token=True)
                hrd_free[sl] = xq3_last
            for m in range(DCH):
                csl = m % 2
                d_w = E("G", (lambda m_, s_: lambda e: e.dma_start(
                    out=w3qc[s_], in_=w3q_sc.ap()[m_]
                    .rearrange("(c p) q -> p c q", p=128)))(m, csl),
                    deps=J([w3qc_free[csl]], w3_written[-1:], w3_gate), dma=True)
                osl = m % 2
                d_xs = E("A", (lambda m_, th_, o_: lambda e: e.dma_start(
                    out=xst[o_], in_=(x_in if call == 0 else x2_sc).ap()
                    [m_ * 128:(m_ + 1) * 128, th_ * TH:(th_ + 1) * TH]))(m, th, osl),
                    deps=J([xst_free[osl]], w3_gate if (m < 2 and th == 0) else []),
                    dma=True)
                bank = 6 + m % 2
                mm = None
                for hc in range(HCH):
                    dep = [d_w, xq3_last]
                    if hc == 0:
                        dep.append(bank_free[bank])
                    mm = E("T", (lambda bk, hc_, s_: lambda t: t.matmul(
                        pb[bk][:, 0:TH], w3qc[s_][:, hc_ * 128:(hc_ + 1) * 128],
                        xq3[:, hc_ * TH:(hc_ + 1) * TH],
                        start=(hc_ == 0), stop=(hc_ == HCH - 1)))(bank, hc, csl),
                        deps=dep, token=(hc == HCH - 1))
                w3qc_free[csl] = mm
                t_sc2 = E("V", (lambda bk, th_: lambda v: v.tensor_tensor(
                    out=evo, in0=pb[bk][:, 0:TH],
                    in1=q3i_bc[:, th_ * TH:(th_ + 1) * TH], op=ALU.mult))(bank, th),
                    deps=J([mm, t_q3i], evo_free), token=True)
                bank_free[bank] = t_sc2
                t_up = E("V", (lambda o_: lambda v: v.tensor_tensor(
                    out=ost[o_], in0=xst[o_], in1=evo, op=ALU.add))(osl),
                    deps=J([t_sc2, d_xs], ost_free[osl]), token=True)
                evo_free[0] = t_up
                if not final_dma:
                    t_x1d = E("S", (lambda m_, th_, o_: lambda e: e.dma_start(
                        out=x1_sc.ap()[m_ * 128:(m_ + 1) * 128,
                                       th_ * TH:(th_ + 1) * TH], in_=ost[o_]))(
                        m, th, osl), deps=[t_up], dma=True)
                    t_cast = E("V", (lambda m_, th_, o_: lambda v: v.tensor_copy(
                        out=xres[:, m_ * TL + th_ * TH:m_ * TL + (th_ + 1) * TH],
                        in_=ost[o_]))(m, th, osl), deps=[t_up], token=True)
                    ost_free[osl] = [t_x1d, t_cast]
                    xst_free[osl] = t_up
                    out_toks.append(t_cast)
                    out_toks.append(t_x1d)
                else:
                    t_od = E("S", (lambda m_, th_, o_: lambda e: e.dma_start(
                        out=out_t.ap()[m_ * 128:(m_ + 1) * 128,
                                       th_ * TH:(th_ + 1) * TH], in_=ost[o_]))(
                        m, th, osl), deps=[t_up], dma=True)
                    ost_free[osl] = [t_od]
                    xst_free[osl] = t_up
                    out_toks.append(t_od)
            xq3_war = list(out_toks[-DCH:])
        return out_toks

    def mamba_sublayer(entry_deps, gate):
        deps0 = J(entry_deps)
        AR.reset()
        XGN = max(DCH * TG // 4, GS * XW, (TCL // 128) * D * 2)
        xg_raw = AR.alloc(BF16, XGN)
        xgq = xg_raw[:, 0:DCH * (TG // 4)]
        xse = xg_raw[:, 0:GS * XW]
        attm = xg_raw[:, 0:(TCL // 128) * D * 2].bitcast(F32)
        outp = xg_raw[:, 0:D * 2].bitcast(F32)
        winb = AR.alloc(BF16, DCH * 2 * DSL)
        stat1m = AR.alloc(F32, TCL)
        stat2m = AR.alloc(F32, TCL)
        mu2 = AR.alloc(F32, TCL)
        rstd2 = AR.alloc(F32, TCL)
        rho2 = AR.alloc(F32, TCL)
        SQB = max(2 * DCH * TCL * 2, TG * 4 + 2 * TG * 2)
        sq_raw = AR.raw(SQB)
        x1h = AR.alloc(F32, DCH * 128)
        sqt = sq_raw[:, 0:DCH * TCL * 2].bitcast(BF16)
        xnq = sq_raw[:, DCH * TCL * 2:DCH * TCL * 4].bitcast(BF16)
        dAt = sq_raw[:, 0:TG * 4].bitcast(F32)
        dbu = sq_raw[:, TG * 4:TG * 6].bitcast(BF16)
        hsc_t = sq_raw[:, TG * 6:TG * 8].bitcast(BF16)
        xs_sl = AR.alloc(BF16, GS * TG)
        du_sl = xs_sl
        dl_raw = AR.raw(max(GS * TG * 2, GS * D * 2))
        delta_sl = dl_raw[:, 0:GS * TG * 2].bitcast(BF16)
        woutb = dl_raw[:, 0:GS * D * 2].bitcast(BF16)
        arow = dl_raw[:, 0:TCL * 4].bitcast(F32)
        brow = dl_raw[:, TCL * 4:2 * TCL * 4].bitcast(F32)
        cv_raw = AR.raw(TG * 8)
        convacc = cv_raw[:, 0:TG * 4].bitcast(F32)
        dtr = cv_raw[:, 0:TG * 4].bitcast(F32)
        bbn = cv_raw[:, 0:TG * 4].bitcast(F32)
        cbn = cv_raw[:, TG * 4:TG * 8].bitcast(F32)
        u_sl = AR.alloc(BF16, GS * TG)
        ybar = u_sl
        sres = AR.alloc(BF16, GS * TG)
        NYB = max(1, GS - 2) * TG
        if GS > 2 and NYB * 4 <= DCH * 2 * DSL * 2:
            yacc23 = winb[:, 0:NYB * 2].bitcast(F32)
        else:
            yacc23 = AR.alloc(F32, NYB)
        edel = sq_raw[:, 0:PSB * 4].bitcast(F32)

        prev = dict(xgq=None, sres=None, u=None, delta=None, du=None,
                    xln=None, ybar=None, dAt=None, dbu=None, h=None,
                    bbn=None, cbn=None, attm=None, outp=None,
                    ag=None, ar=None, rs=None, dtr=None, scan_done=None,
                    winb=None, wout=None, sqstats=None)
        bank_free = [None] * 8
        last_resid = list(deps0)

        for c in range(NCH):
            gdep = gate if c == 0 else []
            xsl = xres.rearrange("p (d t) -> p d t", d=DCH)[
                :, :, c * TCL:(c + 1) * TCL]
            t_sq = E("A", (lambda x_: lambda a: a.activation(
                out=sqt.rearrange("p (d t) -> p d t", d=DCH), in_=x_,
                func=AF.Square))(xsl),
                deps=J(gdep, [prev["sqstats"]]), token=True)
            t_rx = E("V", (lambda x_: lambda v: v.tensor_reduce(
                out=stat1m, in_=x_.rearrange("p d t -> p t d"),
                axis=mybir.AxisListType.X, op=ALU.add))(xsl),
                deps=J(gdep, [prev["sqstats"]]), token=True, strict=True)
            t_rx2 = E("V", lambda v: v.tensor_reduce(
                out=stat2m, in_=sqt.rearrange("p (d t) -> p t d", d=DCH),
                axis=mybir.AxisListType.X, op=ALU.add), deps=[t_sq], token=True,
                strict=True)
            t_px = sum_bc(stat1m, stat1m, TCL, [t_rx], (0,), bank_free)
            t_px2 = sum_bc(stat2m, stat2m, TCL, [t_rx2], (1,), bank_free)
            t_mu = E("V", lambda v: v.tensor_scalar_mul(
                out=mu2, in0=stat1m, scalar1=1.0 / D), deps=[t_px], token=True)
            t_m2 = E("V", lambda v: v.tensor_tensor(
                out=stat1m, in0=mu2, in1=mu2, op=ALU.mult), deps=[t_mu], token=True)
            t_var = E("V", lambda v: v.scalar_tensor_tensor(
                out=stat2m, in0=stat2m, scalar=1.0 / D, in1=stat1m,
                op0=ALU.mult, op1=ALU.subtract), deps=[t_px2, t_m2], token=True)
            t_rstd = rsqrt_bc(rstd2, stat2m, 1.0, EPS_LN, [t_var])
            t_r2 = E("V", lambda v: v.tensor_tensor(
                out=rho2, in0=rstd2, in1=rstd2, op=ALU.mult), deps=[t_rstd],
                token=True)
            t_r3 = E("V", lambda v: v.tensor_scalar(
                out=rho2, in0=rho2, scalar1=-EPS_LN, scalar2=1.0,
                op0=ALU.mult, op1=ALU.add), deps=[t_r2], token=True)
            t_rho = rsqrt_bc(rho2, rho2, 1.0, EPS_RMS_A, [t_r3])
            t_rr = E("V", lambda v: v.tensor_tensor(
                out=rho2, in0=rho2, in1=rstd2, op=ALU.mult), deps=[t_rho, t_rstd],
                token=True)
            t_x1 = E("V", (lambda x_: lambda v: v.tensor_tensor(
                out=sqt.rearrange("p (d t) -> p d t", d=DCH), in0=x_,
                in1=mu2.unsqueeze(1).broadcast_to((128, DCH, TCL)),
                op=ALU.subtract))(xsl), deps=[t_mu, t_rx2], token=True)
            t_n1 = E("V", lambda v: v.tensor_tensor(
                out=xnq.rearrange("p (d t) -> p d t", d=DCH),
                in0=sqt.rearrange("p (d t) -> p d t", d=DCH),
                in1=rho2.unsqueeze(1).broadcast_to((128, DCH, TCL)),
                op=ALU.mult), deps=[t_x1, t_rr], token=True)
            t_n2 = E("V", lambda v: v.tensor_tensor(
                out=xnq.rearrange("p (d t) -> p d t", d=DCH),
                in0=xnq.rearrange("p (d t) -> p d t", d=DCH),
                in1=wnorm_s.unsqueeze(2).broadcast_to((128, DCH, TCL)),
                op=ALU.mult), deps=[t_n1], token=True)
            t_agw = E("S", lambda e: e.dma_start(
                out=ag_in.ap().rearrange("(d p) t -> p d t", p=128), in_=xnq),
                deps=J([t_n2], [prev["ag"]]), dma=True)
            t_ag = E("G", lambda g: g.collective_compute(
                "AllGather", ALU.bypass, replica_groups=RG,
                ins=[ag_in.ap().opt()], outs=[ag_out.ap().opt()]),
                deps=[t_agw], cc=True)
            prev["ag"] = t_ag
            if DEBUG and c == 0:
                prev["ag"] = J([t_ag],
                               [E("G", lambda g: g.dma_start(
                                   out=aidump.ap(), in_=ag_in.ap()),
                                   deps=[t_agw], dma=True)],
                               [E("G", lambda g: g.dma_start(
                                   out=agdump.ap(), in_=ag_out.ap()),
                                   deps=[t_ag], dma=True)])
                t_ag = prev["ag"]

            t_win = E("G", lambda g: g.dma_start(
                out=winb, in_=win_bf.ap().rearrange("(c p) o -> p c o", p=128)),
                deps=J([prev["winb"]], gdep), dma=True)

            TQ = TG // 4
            evac_toks = []
            for tq in range(4):
                NRQ = TQ // TCL
                d_x = None
                for rr in range(NRQ):
                    d_x = E("S", (lambda tq_, rr_: lambda e: e.dma_start(
                        out=xgq.rearrange("p (d r t) -> p d r t", d=DCH, r=NRQ)
                        [:, :, rr_:rr_ + 1, :].squeeze(2),
                        in_=ag_out.ap()
                        .rearrange("(r d p) t -> p d r t", p=128, d=DCH)
                        [:, :, NRQ * tq_ + rr_:NRQ * tq_ + rr_ + 1, :].squeeze(2)))(tq, rr),
                        deps=J([t_ag],
                               evac_toks[-1:] if evac_toks else [prev["xgq"]]),
                        dma=True)
                for m in range(2 * GS):
                    bank = m % 8
                    mm = None
                    for kc in range(DCH):
                        dep = [d_x, t_win]
                        if kc == 0:
                            dep.append(bank_free[bank])
                        mm = E("T", (lambda bk, kc_, m_: lambda t: t.matmul(
                            pb[bk],
                            winb[:, kc_ * 2 * DSL + m_ * 128:
                                 kc_ * 2 * DSL + (m_ + 1) * 128],
                            xgq[:, kc_ * TQ:(kc_ + 1) * TQ],
                            start=(kc_ == 0), stop=(kc_ == DCH - 1)))(bank, kc, m),
                            deps=dep, token=(kc == DCH - 1))
                    if m < GS:
                        t_e = E("A", (lambda bk, m_, tq_: lambda a: a.activation(
                            out=xs_sl[:, m_ * TG + tq_ * TQ:m_ * TG + (tq_ + 1) * TQ],
                            in_=pb[bk], func=AF.Copy))(bank, m, tq),
                            deps=J([mm], [prev["scan_done"]]
                                   if (m == 0 and tq == 0) else []), token=True)
                    else:
                        t_e = E("A", (lambda bk, m_, tq_: lambda a: a.activation(
                            out=sres[:, (m_ - GS) * TG + tq_ * TQ:
                                     (m_ - GS) * TG + (tq_ + 1) * TQ],
                            in_=pb[bk], func=AF.Silu))(bank, m, tq),
                            deps=J([mm], [prev["sres"]]
                                   if (m == GS and tq == 0) else []), token=True)
                    bank_free[bank] = t_e
                    evac_toks.append(t_e)

            if DEBUG and c == 0:
                evac_toks.append(E("G", lambda g: g.dma_start(
                    out=xsdump.ap(), in_=xs_sl), deps=evac_toks[-1:], dma=True))
            conv_toks = []
            t_xse_last = None
            for g in range(GS):
                t_c0 = E("V", (lambda g_: lambda v: v.tensor_copy(
                    out=xse[:, g_ * XW:g_ * XW + KCV - 1],
                    in_=xs_tail[:, g_ * (KCV - 1):(g_ + 1) * (KCV - 1)]))(g),
                    deps=J(evac_toks[-1:], [t_tail0],
                           [prev["attm"], prev["outp"]] if g == 0 else []),
                    token=True)
                t_c1 = E("V", (lambda g_: lambda v: v.tensor_copy(
                    out=xse[:, g_ * XW + KCV - 1:(g_ + 1) * XW],
                    in_=xs_sl[:, g_ * TG:(g_ + 1) * TG]))(g),
                    deps=evac_toks[-1:], token=True)
                t_tl = E("V", (lambda g_: lambda v: v.tensor_copy(
                    out=xs_tail[:, g_ * (KCV - 1):(g_ + 1) * (KCV - 1)],
                    in_=xs_sl[:, (g_ + 1) * TG - (KCV - 1):(g_ + 1) * TG]))(g),
                    deps=evac_toks[-1:], token=True)
                t_acc = E("V", (lambda g_: lambda v: v.tensor_single_scalar(
                    out=convacc, in_=xse[:, g_ * XW:g_ * XW + TG],
                    scalar=convw_s[:, g_ * KCV:g_ * KCV + 1], op=ALU.mult))(g),
                    deps=J([t_c0, t_c1],
                           conv_toks[-1:] if g > 0 else
                           [prev["dtr"], prev["bbn"], prev["cbn"]]), token=True)
                for k in range(1, KCV):
                    t_acc = E("V", (lambda g_, k_: lambda v: v.scalar_tensor_tensor(
                        out=convacc, in0=xse[:, g_ * XW + k_:g_ * XW + k_ + TG],
                        scalar=convw_s[:, g_ * KCV + k_:g_ * KCV + k_ + 1],
                        in1=convacc, op0=ALU.mult, op1=ALU.add))(g, k),
                        deps=[t_acc], token=True)
                t_u = E("A", (lambda g_: lambda a: a.activation(
                    out=u_sl[:, g_ * TG:(g_ + 1) * TG], in_=convacc, func=AF.Silu,
                    bias=convb_s[:, g_:g_ + 1]))(g),
                    deps=J([t_acc], [prev["u"]] if g == 0 else []), token=True)
                conv_toks.append(t_u)
                t_xse_last = t_acc

            if DEBUG and c == 0:
                conv_toks.append(E("G", lambda g: g.dma_start(
                    out=u2dump.ap(), in_=u_sl), deps=conv_toks, dma=True))
                conv_toks.append(E("G", lambda g: g.dma_start(
                    out=sr2dump.ap(), in_=sres), deps=conv_toks, dma=True))
            xp_toks = []
            t_ed_prev = None
            for nb in range(TG // PSB):
                for (m0, msz, bank0) in ((0, RK, 0), (RK, 2 * NST, 4)):
                    bank = bank0 + nb % 4
                    mm = None
                    for g in range(GS):
                        dep = [conv_toks[g], t_wxp]
                        if g == 0:
                            dep.append(bank_free[bank])
                        mm = E("T", (lambda bk, g_, m0_, ms_, nb_: lambda t: t.matmul(
                            pb[bk][0:ms_, :],
                            wxp_b[:, g_ * XPW + m0_:g_ * XPW + m0_ + ms_],
                            u_sl[:, g_ * TG + nb_ * PSB:g_ * TG + (nb_ + 1) * PSB],
                            start=(g_ == 0), stop=(g_ == GS - 1)))(
                            bank, g, m0, msz, nb), deps=dep, token=(g == GS - 1))
                    t_ev = E("A", (lambda bk, ms_: lambda a: a.activation(
                        out=edel[0:ms_, :], in_=pb[bk][0:ms_, :], func=AF.Copy))(
                        bank, msz), deps=J([mm], [t_ed_prev]), token=True)
                    bank_free[bank] = t_ev
                    t_d = E("S", (lambda m0_, ms_, nb_: lambda e: e.dma_start(
                        out=ar_in.ap()[m0_:m0_ + ms_, nb_ * PSB:(nb_ + 1) * PSB],
                        in_=edel[0:ms_, :]))(m0, msz, nb),
                        deps=J([t_ev], [prev["ar"]]), dma=True)
                    t_ed_prev = t_d
                    xp_toks.append(t_d)
            t_arc = E("G", lambda g: g.collective_compute(
                "AllReduce", ALU.add, replica_groups=RG,
                ins=[ar_in.ap().opt()], outs=[ar_out.ap().opt()]),
                deps=xp_toks, cc=True)
            prev["ar"] = t_arc
            if DEBUG and c == 0:
                prev["ar"] = J([t_arc],
                               [E("G", lambda g: g.dma_start(
                                   out=xpdump.ap(), in_=ar_in.ap()),
                                   deps=[t_arc], dma=True)],
                               [E("G", lambda g: g.dma_start(
                                   out=xodump.ap(), in_=ar_out.ap()),
                                   deps=[t_arc], dma=True)])
                t_arc = prev["ar"]

            t_dtr = E("G", lambda g: g.dma_start(out=dtr, in_=ar_out.ap()[0:RK, :]),
                      deps=J([t_arc], conv_toks[-1:], [prev["du"]]), dma=True)
            prev["dtr"] = t_dtr
            del_toks = []
            for m in range(GS):
                for nb in range(TG // PSB):
                    bank = (m * (TG // PSB) + nb) % 8
                    mm = E("T", (lambda bk, m_, nb_: lambda t: t.matmul(
                        pb[bk], wdt_b[:, m_ * 128:(m_ + 1) * 128],
                        dtr[:, nb_ * PSB:(nb_ + 1) * PSB], start=True, stop=True))(
                        bank, m, nb), deps=[t_dtr, t_wdt, bank_free[bank]],
                        token=True)
                    t_e = E("A", (lambda bk, m_: lambda a: a.activation(
                        out=edel, in_=pb[bk], func=AF.Exp,
                        bias=bdt_s[:, m_:m_ + 1]))(bank, m),
                        deps=J([mm], [t_ed_prev]), token=True)
                    t_ed_prev = None
                    bank_free[bank] = t_e
                    t_l = E("A", (lambda m_, nb_: lambda a: a.activation(
                        out=delta_sl[:, m_ * TG + nb_ * PSB:m_ * TG + (nb_ + 1) * PSB],
                        in_=edel, func=AF.Ln, bias=1.0))(m, nb),
                        deps=J([t_e], [prev["wout"]] if (m == 0 and nb == 0) else []),
                        token=True)
                    del_toks.append(t_l)
            prev["delta"] = del_toks[-1]
            if c == 0:
                DBG(xs_sl[:, 0:8], evac_toks[-1:])
                DBG(sres[:, 0:8], evac_toks[-1:])
                DBG(u_sl[:, 0:8], conv_toks[-1:])
                DBG(dtr[:, 0:8], [t_dtr])
                DBG(delta_sl[:, 0:8], del_toks[-1:])
            t_du = E("V", lambda v: v.tensor_tensor(
                out=du_sl, in0=delta_sl, in1=u_sl, op=ALU.mult),
                deps=J(del_toks[-1:], conv_toks[-1:],
                       [t_xse_last], evac_toks[-1:]), token=True)
            prev["du"] = t_du

            yz_toks = []
            for bk in range(min(8, 4 * min(GS, 2))):
                yz_toks.append(E("V", (lambda b_: lambda v: v.memset(
                    pb[b_], 0.0))(bk), deps=J([bank_free[bk]], gdep), token=True))
                bank_free[bk] = None
            ydeps = {}
            for g in range(GS):
                if g < 2:
                    ydeps[g] = yz_toks[4 * g:4 * g + 4]
            if GS > 2:
                t_yz2 = E("V", lambda v: v.memset(yacc23, 0.0),
                          deps=J([prev["ybar"]], evac_toks[-1:], gdep),
                          token=True)
                for g in range(2, GS):
                    ydeps[g] = [t_yz2]
            st_last = None
            for n in range(NST):
                d_b = E("G", (lambda n_: lambda g: g.dma_start(
                    out=bbn, in_=ar_out.ap()[RK + n_:RK + n_ + 1, :]
                    .broadcast_to((128, TG))))(n),
                    deps=J([t_arc], [prev["bbn"] if n > 0 else t_du]), dma=True)
                d_c = E("G", (lambda n_: lambda g: g.dma_start(
                    out=cbn, in_=ar_out.ap()[RK + NST + n_:RK + NST + n_ + 1, :]
                    .broadcast_to((128, TG))))(n),
                    deps=J([t_arc], [prev["cbn"] if n > 0 else conv_toks[-1]]),
                    dma=True)
                for g in range(GS):
                    gn = g * NST + n
                    t_da = E("A", (lambda g_, gn_: lambda a: a.activation(
                        out=dAt, in_=delta_sl[:, g_ * TG:(g_ + 1) * TG],
                        func=AF.Exp, scale=acoef[:, gn_:gn_ + 1]))(g, gn),
                        deps=J(del_toks[-1:], [t_acoef], [prev["dAt"]],
                               [prev["ag"], t_n2] if (n == 0 and g == 0) else []),
                        token=True)
                    t_db = E("V", (lambda g_: lambda v: v.tensor_tensor(
                        out=dbu, in0=du_sl[:, g_ * TG:(g_ + 1) * TG], in1=bbn,
                        op=ALU.mult))(g), deps=J([t_du, d_b], [prev["dbu"]]),
                        token=True)
                    t_scn = E("V", (lambda gn_: lambda v: v.tensor_tensor_scan(
                        out=hsc_t, data0=dAt, data1=dbu,
                        initial=state[:, gn_:gn_ + 1], op0=ALU.mult, op1=ALU.add))(gn),
                        deps=J([t_da, t_db], [st_last], [prev["h"]]), token=True,
                        strict=True)
                    prev["dAt"] = t_scn
                    prev["dbu"] = t_scn
                    st_last = E("V", (lambda gn_: lambda v: v.tensor_copy(
                        out=state[:, gn_:gn_ + 1], in_=hsc_t[:, TG - 1:TG]))(gn),
                        deps=[t_scn], token=True)
                    t_ym = E("V", lambda v: v.tensor_tensor(
                        out=hsc_t, in0=hsc_t, in1=cbn, op=ALU.mult),
                        deps=[t_scn, d_c], token=True)
                    if g < 2:
                        accs = []
                        for q in range(4):
                            accs.append(E("V", (lambda g_, q_: lambda v:
                                v.tensor_tensor(
                                    out=pb[4 * g_ + q_], in0=pb[4 * g_ + q_],
                                    in1=hsc_t[:, q_ * PSB:(q_ + 1) * PSB],
                                    op=ALU.add))(g, q),
                                deps=J([t_ym], ydeps[g]), token=True))
                        ydeps[g] = accs
                        t_ya = accs[-1]
                    else:
                        t_ya = E("V", (lambda g_: lambda v: v.tensor_tensor(
                            out=yacc23[:, (g_ - 2) * TG:(g_ - 1) * TG],
                            in0=yacc23[:, (g_ - 2) * TG:(g_ - 1) * TG],
                            in1=hsc_t, op=ALU.add))(g), deps=J([t_ym], ydeps[g]),
                            token=True)
                        ydeps[g] = [t_ya]
                    prev["h"] = t_ya
                    prev["bbn"] = t_db
                    prev["cbn"] = t_ya
            prev["scan_done"] = prev["h"]
            prev["sqstats"] = J([prev["dAt"]], [prev["h"]])

            t_wout = E("G", lambda g: g.dma_start(
                out=woutb, in_=wout_bf.ap().rearrange("(g p) o -> p g o", p=128)),
                deps=J([prev["scan_done"]], [prev["wout"]]), dma=True)

            yb_toks = []
            for g in range(GS):
                if g < 2:
                    q_toks = []
                    for q in range(4):
                        q_toks.append(E("V", (lambda g_, q_: lambda v:
                            v.scalar_tensor_tensor(
                                out=pb[4 * g_ + q_],
                                in0=u_sl[:, g_ * TG + q_ * PSB:
                                         g_ * TG + (q_ + 1) * PSB],
                                scalar=dvec_s[:, g_:g_ + 1], in1=pb[4 * g_ + q_],
                                op0=ALU.mult, op1=ALU.add))(g, q),
                            deps=J(ydeps[g], [t_du]), token=True))
                    t_yb = None
                    for q in range(4):
                        t_yb = E("V", (lambda g_, q_: lambda v: v.tensor_tensor(
                            out=ybar[:, g_ * TG + q_ * PSB:g_ * TG + (q_ + 1) * PSB],
                            in0=pb[4 * g_ + q_],
                            in1=sres[:, g_ * TG + q_ * PSB:g_ * TG + (q_ + 1) * PSB],
                            op=ALU.mult))(g, q),
                            deps=J([q_toks[q]], [t_du] if q == 0 else []),
                            token=True)
                        bank_free[4 * g + q] = t_yb
                    yb_toks.append(t_yb)
                else:
                    t_t = E("V", (lambda g_: lambda v: v.scalar_tensor_tensor(
                        out=yacc23[:, (g_ - 2) * TG:(g_ - 1) * TG],
                        in0=u_sl[:, g_ * TG:(g_ + 1) * TG],
                        scalar=dvec_s[:, g_:g_ + 1],
                        in1=yacc23[:, (g_ - 2) * TG:(g_ - 1) * TG],
                        op0=ALU.mult, op1=ALU.add))(g),
                        deps=J(ydeps[g], [t_du]), token=True)
                    t_yb = E("V", (lambda g_: lambda v: v.tensor_tensor(
                        out=ybar[:, g_ * TG:(g_ + 1) * TG],
                        in0=yacc23[:, (g_ - 2) * TG:(g_ - 1) * TG],
                        in1=sres[:, g_ * TG:(g_ + 1) * TG], op=ALU.mult))(g),
                        deps=[t_t], token=True)
                    yb_toks.append(t_yb)
            if c == 0:
                DBG(bbn[:, 0:8], [prev["bbn"]])
                DBG(yacc23[:, 0:8], list(ydeps[max(ydeps)]))
                DBG(ybar[:, 0:8], yb_toks[-1:])
            if DEBUG and c == 0:
                yb_toks.append(E("G", lambda g: g.dma_start(
                    out=ybdump.ap(), in_=ybar), deps=yb_toks[-1:], dma=True))
                yb_toks.append(E("G", lambda g: g.dma_start(
                    out=uddump.ap(), in_=du_sl), deps=yb_toks[-1:], dma=True))
            prev["ybar"] = yb_toks[-1]
            prev["sres"] = yb_toks[-1]
            prev["u"] = yb_toks[-1]

            rs_toks = []
            for tt in range(TG // 128):
                ev_toks = []
                for nb in range(NBW):
                    bank = (tt % 2) * 4 + nb % 4
                    mm = None
                    for g in range(GS):
                        dep = [yb_toks[g], t_wout]
                        if g == 0:
                            dep.append(bank_free[bank])
                        mm = E("T", (lambda bk, g_, tt_, nb_: lambda t: t.matmul(
                            pb[bk][:, 0:WW],
                            ybar[:, g_ * TG + tt_ * 128:g_ * TG + (tt_ + 1) * 128],
                            woutb[:, g_ * D + nb_ * WW:g_ * D + (nb_ + 1) * WW],
                            start=(g_ == 0), stop=(g_ == GS - 1)))(bank, g, tt, nb),
                            deps=dep, token=(g == GS - 1))
                    t_ev = E("A", (lambda bk, nb_: lambda a: a.activation(
                        out=outp[:, nb_ * WW:(nb_ + 1) * WW], in_=pb[bk][:, 0:WW],
                        func=AF.Copy))(bank, nb),
                        deps=J([mm], [prev["outp"], t_xse_last] if nb == 0 else []),
                        token=True)
                    bank_free[bank] = t_ev
                    ev_toks.append(t_ev)
                t_d = E("S", (lambda tt_: lambda e: e.dma_start(
                    out=rs_in.ap()[tt_ * 128:(tt_ + 1) * 128, :], in_=outp))(tt),
                    deps=J(ev_toks, [prev["rs"]]), dma=True)
                prev["outp"] = t_d
                rs_toks.append(t_d)
            t_rs = E("G", lambda g: g.collective_compute(
                "ReduceScatter", ALU.add, replica_groups=RG,
                ins=[rs_in.ap().opt()], outs=[rs_out.ap().opt()]),
                deps=rs_toks, cc=True)
            prev["rs"] = t_rs
            d_rb = E("S", lambda e: e.dma_start(
                out=attm, in_=rs_out.ap().rearrange("(b p) d -> p b d", p=128)),
                deps=J([t_rs], [prev["attm"], prev["outp"]]), dma=True)
            if DEBUG and c == 0:
                t_rs = J([t_rs],
                         [E("G", lambda g: g.dma_start(
                             out=ridump.ap(), in_=rs_in.ap()), deps=[t_rs],
                             dma=True)],
                         [E("G", lambda g: g.dma_start(
                             out=rodump.ap(), in_=rs_out.ap()), deps=[t_rs],
                             dma=True)])
                prev["rs"] = t_rs
            t_ar = E("V", lambda v: v.tensor_scalar_add(
                out=arow, in0=rstd2, scalar1=1.0),
                deps=J([t_rstd, t_m2], ev_toks[-1:]), token=True)
            t_br = E("V", lambda v: v.tensor_tensor(
                out=brow, in0=mu2, in1=rstd2, op=ALU.mult),
                deps=[t_mu, t_rstd, t_var, t_ar], token=True)
            x1h_free = []
            casts = []
            d2s = []
            t_dbg1 = t_dbg2 = None
            if DEBUG and c == 0:
                t_dbg1 = DBG(attm[:, 0:8], [d_rb])
            for bt in range(TCL // 128):
                d_x1 = E("S", (lambda c_, b_: lambda e: e.dma_start(
                    out=x1h.rearrange("p (d t) -> p d t", d=DCH),
                    in_=x1_sc.ap().rearrange("(d p) t -> p d t", p=128)
                    [:, :, c_ * TCL + b_ * 128:c_ * TCL + (b_ + 1) * 128]))(c, bt),
                    deps=x1h_free, dma=True)
                t_a1 = E("V", (lambda b_: lambda v: v.tensor_tensor(
                    out=x1h.rearrange("p (d t) -> p d t", d=DCH),
                    in0=x1h.rearrange("p (d t) -> p d t", d=DCH),
                    in1=arow[:, b_ * 128:(b_ + 1) * 128].unsqueeze(1)
                    .broadcast_to((128, DCH, 128)), op=ALU.mult))(bt),
                    deps=[d_x1, t_ar], token=True)
                t_a2m = E("V", (lambda b_: lambda v: v.tensor_tensor(
                    out=x1h.rearrange("p (d t) -> p d t", d=DCH),
                    in0=x1h.rearrange("p (d t) -> p d t", d=DCH),
                    in1=brow[:, b_ * 128:(b_ + 1) * 128].unsqueeze(1)
                    .broadcast_to((128, DCH, 128)), op=ALU.subtract))(bt),
                    deps=[t_a1, t_br], token=True)
                add_toks = []
                for dcc in range(DCH):
                    bank = 6 + dcc % 2
                    t_tr = E("T", (lambda b_, d_, bk: lambda t: t.transpose(
                        pb[bk][:, 0:128],
                        attm[:, b_ * D + d_ * 128:b_ * D + (d_ + 1) * 128],
                        identf))(bt, dcc, bank),
                        deps=J([d_rb, t_id], [bank_free[bank]]), token=True)
                    t_ad = E("V", (lambda d_, bk: lambda v: v.tensor_tensor(
                        out=x1h[:, d_ * 128:(d_ + 1) * 128],
                        in0=x1h[:, d_ * 128:(d_ + 1) * 128],
                        in1=pb[bk][:, 0:128], op=ALU.add))(dcc, bank),
                        deps=[t_tr, t_a2m], token=True)
                    bank_free[bank] = t_ad
                    add_toks.append(t_ad)
                d_x2 = E("S", (lambda c_, b_: lambda e: e.dma_start(
                    out=x2_sc.ap().rearrange("(d p) t -> p d t", p=128)
                    [:, :, c_ * TCL + b_ * 128:c_ * TCL + (b_ + 1) * 128],
                    in_=x1h.rearrange("p (d t) -> p d t", d=DCH)))(c, bt),
                    deps=add_toks, dma=True)
                t_cast = E("V", (lambda c_, b_: lambda v: v.tensor_copy(
                    out=xres.rearrange("p (d t) -> p d t", d=DCH)
                    [:, :, c_ * TCL + b_ * 128:c_ * TCL + (b_ + 1) * 128],
                    in_=x1h.rearrange("p (d t) -> p d t", d=DCH)))(c, bt),
                    deps=add_toks, token=True)
                if DEBUG and c == 0 and bt == 0:
                    t_dbg2 = DBG(x1h[:, 0:8], add_toks)
                x1h_free = J([t_cast, d_x2], [t_dbg2])
                casts.append(t_cast)
                d2s.append(d_x2)
            prev["attm"] = J(casts, [t_dbg1])
            prev["xln"] = t_n1
            prev["xgq"] = J(casts, [t_dbg1])
            prev["winb"] = yb_toks[-1]
            prev["wout"] = J(ev_toks[-1:], [t_a2m])
            last_resid = J(casts, d2s, [t_dbg2])
        return last_resid

    entry = [t_idb]
    g0 = sch.barrier_tokens()
    ffn1_out = ffn_sublayer(0, entry, g0)
    DBG(xres[:, 0:8], ffn1_out)
    g1 = sch.barrier_tokens()
    mam_out = mamba_sublayer(ffn1_out, g1)
    DBG(xres[:, 0:8], mam_out)
    g2 = sch.barrier_tokens()
    ffn2_out = ffn_sublayer(1, mam_out, g2, final_dma=True)
    if DEBUG:
        E("S", lambda e: e.dma_start(out=dbg_t.ap()[:, 0:96], in_=dbg_sb),
          deps=J(ffn2_out, sch.barrier_tokens()), dma=True)
    E("S", lambda e: e.nop(), deps=J(ffn2_out, sch.drain_tokens()))

    with nc.Block() as block:
        sem_stack = sch.finalize(nc, block)
    sem_stack.close()
    stack.close()
    return nc


_BUILT = {}


def _get_nc():
    if "real" not in _BUILT:
        _BUILT["real"] = build(REAL_CFG)
    return _BUILT["real"]


def stage_inputs(cfg, core, inp):
    D = cfg["D"]; DI = cfg["DI"]; NST = cfg["NST"]; KCV = cfg["KCV"]
    L = cfg["L"]; H = cfg["H"]; NC = cfg["NC"]; TG = cfg["TG"]
    TL = L // NC; DSL = DI // NC; NCH = L // TG; TCL = TG // NC
    GS = DSL // 128; DCH = D // 128; HCH = H // 128
    c = core
    ca = np.ascontiguousarray
    x = np.asarray(inp["L0_attn_x"])[0]
    xs = x.reshape(NCH, NC, TCL, D)[:, c].reshape(TL, D)
    w1t = ca(np.asarray(inp["L0_ffn_w1"]).T)
    w2t = ca(np.asarray(inp["L0_ffn_w2"]).T)
    w3t = ca(np.asarray(inp["L0_ffn_w3"]).T)
    W_in = np.asarray(inp["L0_attn_W_in"])
    sl = slice(c * DSL, (c + 1) * DSL)
    d = {
        "x_sl": ca(xs.T),
        "w1t": w1t, "w2t": w2t, "w3t": w3t,
        "s1rows": ca(w1t[c * (D // NC):(c + 1) * (D // NC)]),
        "s2rows": ca(w2t[c * (D // NC):(c + 1) * (D // NC)]),
        "s3rows": ca(w3t[c * (H // NC):(c + 1) * (H // NC)]),
        "winsl": ca(np.concatenate(
            [W_in[:, sl], W_in[:, DI + c * DSL:DI + (c + 1) * DSL]], axis=1)),
        "woutsl": ca(np.asarray(inp["L0_attn_W_out"])[sl]),
        "wxpsl": ca(np.asarray(inp["L0_attn_W_xp"])[sl]),
        "wdtsl": ca(np.asarray(inp["L0_attn_W_dt"])[:, sl]),
        "convwsl": ca(np.asarray(inp["L0_attn_conv_w"])[:, sl].T
                      .reshape(GS, 128, KCV).transpose(1, 0, 2)
                      .reshape(128, GS * KCV)),
        "convbsl": ca(np.asarray(inp["L0_attn_conv_b"])[sl].reshape(GS, 128).T),
        "alogsl": ca(np.asarray(inp["L0_attn_A_log"])[sl]
                     .reshape(GS, 128, NST).transpose(1, 0, 2)
                     .reshape(128, GS * NST)),
        "ddsl": ca(np.asarray(inp["L0_attn_D"])[sl].reshape(GS, 128).T),
        "bdtsl": ca(np.asarray(inp["L0_attn_b_dt"])[sl].reshape(GS, 128).T),
        "wnormv": ca(np.asarray(inp["L0_attn_w_norm"]).reshape(DCH, 128).T),
        "g1v": ca(np.asarray(inp["L0_ffn_g1"]).reshape(DCH, 128).T),
        "g2v": ca(np.asarray(inp["L0_ffn_g2"]).reshape(DCH, 128).T),
        "g3v": ca(np.asarray(inp["L0_ffn_g3"]).reshape(HCH, 128).T),
        "identin": np.eye(128, dtype=np.float32),
        "constv": np.tile(np.array(
            [1e-5, 1e-8, 1.0, -0.5, 1.0 / H, -1.0, 0.5, 2.0], np.float32), (128, 1)),
    }
    return {k: np.ascontiguousarray(v, dtype=np.float32) for k, v in d.items()}


def kernel(**inputs):
    cfg = REAL_CFG
    NC = cfg["NC"]; L = cfg["L"]; D = cfg["D"]; TG = cfg["TG"]
    TL = L // NC; NCH = L // TG; TCL = TG // NC
    nc = _get_nc()
    in_maps = [stage_inputs(cfg, c, inputs) for c in range(NC)]
    res = run_bass_kernel_spmd(nc, in_maps, core_ids=list(range(NC)))
    out = np.empty((L, D), dtype=np.float32)
    for c in range(NC):
        o = np.asarray(res.results[c]["out_sl"]).T.reshape(NCH, TCL, D)
        for k in range(NCH):
            out[k * TG + c * TCL:k * TG + (c + 1) * TCL] = o[k]
    return out.reshape(1, L, D)
